# version 08
# baseline (speedup 1.0000x reference)
import numpy as np
import ml_dtypes
import concourse.bass as bass
import concourse.mybir as mybir
from concourse.bass_utils import run_bass_kernel_spmd
from contextlib import ExitStack

F32 = mybir.dt.float32
BF16 = mybir.dt.bfloat16
AF = mybir.ActivationFunctionType
ALU = mybir.AluOpType
AX = mybir.AxisListType

SPLIT = False
DEBUG_DUMP = False
NCORES = 8
N = 2048
D = 4096
HQ, HKV, HD = 32, 8, 128
NT = N // 128
NKC = D // 128
ENG = ("sync", "gpsimd", "tensor", "vector", "scalar")
NEG = -1.0e30


class Sem:
    __slots__ = ("h", "n", "eng")

    def __init__(self, h, eng=None):
        self.h = h
        self.n = 0
        self.eng = eng


class Buf:
    __slots__ = ("name", "writers", "readers")

    def __init__(self, name):
        self.name = name
        self.writers = {}
        self.readers = {}


def _put(d, sem, val):
    k = id(sem)
    old = d.get(k)
    if old is None or old[1] < val:
        d[k] = (sem, val)


class Em:
    def __init__(self, nc, es):
        self.nc = nc
        self.es = es
        self.stream = {e: [] for e in ENG}
        self.nsem = 0
        self.all_sems = []
        self.esem = {}
        for e in ENG:
            self.esem[e] = self.sem(f"E_{e}", eng=e)
        self.waited = {e: {} for e in ENG}
        self.pending = {e: [] for e in ENG}
        self.dsems = {}

    def sem(self, name=None, eng=None):
        self.nsem += 1
        h = self.es.enter_context(self.nc.semaphore(name or f"s{self.nsem}"))
        s = Sem(h, eng)
        self.all_sems.append(s)
        return s

    def _wait(self, eng, sem, val):
        if sem is None or val <= 0:
            return
        if sem.eng == eng and eng == "tensor":
            return
        w = self.waited[eng]
        if w.get(id(sem), 0) >= val:
            return
        w[id(sem)] = val
        self.stream[eng].append(("w", sem, val))

    def op(self, eng, fn, reads=(), writes=(), sem=None, inc=1, waits=()):
        for (s, v) in waits:
            self._wait(eng, s, v)
        for b in reads:
            for (s, v) in b.writers.values():
                self._wait(eng, s, v)
        for b in writes:
            for (s, v) in b.writers.values():
                self._wait(eng, s, v)
            for (s, v) in b.readers.values():
                self._wait(eng, s, v)
            for e2 in ENG:
                if e2 != eng:
                    assert not any(b is pb for pb, _ in self.pending[e2]), \
                        f"write to {b.name} while pending read on {e2}"
        if sem is None and eng in ("vector", "scalar", "gpsimd"):
            sem = "mark"
        use_sem = self.esem[eng] if sem == "mark" else sem
        if use_sem is not None:
            use_sem.n += inc
        self.stream[eng].append(("o", fn, use_sem, inc))
        recs = [(b, "r") for b in reads] + [(b, "w") for b in writes]
        if use_sem is None:
            self.pending[eng].extend(recs)
        else:
            mark = (use_sem, use_sem.n)
            for b, kind in self.pending[eng] + recs:
                _put(b.readers if kind == "r" else b.writers, mark[0], mark[1])
            self.pending[eng] = []

    def dma(self, qeng, out_ap, in_ap, reads=(), writes=(), waits=()):
        key = (writes[0].name if writes else (reads[0].name if reads else qeng))
        dsem = self.dsems.get(key)
        if dsem is None:
            dsem = self.sem("d_" + str(key))
            self.dsems[key] = dsem
        self.op(qeng, (lambda o, i: lambda E: E.dma_start(o, i))(out_ap, in_ap),
                reads=reads, writes=writes, sem=dsem, inc=16, waits=waits)

    def barrier(self, name):
        for e in ENG:
            assert not self.pending[e], f"pending ops at barrier on {e}"
        for e in ENG:
            for s in self.all_sems:
                self._wait(e, s, s.n)
        s = self.sem(name)
        for e in ENG:
            self.op(e, lambda E: E.nop(), sem=s, inc=1)
        tgt = s.n
        for e in ENG:
            self._wait(e, s, tgt)
        for e in ENG:
            for s2 in self.all_sems:
                self.waited[e][id(s2)] = max(self.waited[e].get(id(s2), 0), s2.n)
        return s

    def emit(self):
        with self.nc.Block() as block:
            for e in ENG:
                def make(eng):
                    def body(E):
                        for item in self.stream[eng]:
                            if item[0] == "w":
                                E.wait_ge(item[1].h, item[2])
                            else:
                                inst = item[1](E)
                                if item[2] is not None:
                                    inst.then_inc(item[2].h, item[3])
                    return body
                getattr(block, e)(make(e))


def MM(out, l, r, st, sp):
    return lambda E: E.matmul(out, l, r, start=st, stop=sp, skip_group_check=True)


def TRS(out, in_, iden):
    return lambda E: E.transpose(out, in_, iden)


_CONST = None


def host_constants():
    global _CONST
    if _CONST is not None:
        return _CONST
    bf = ml_dtypes.bfloat16
    ar = np.arange(128)
    c = {}
    c["identb"] = np.eye(128, dtype=np.float32).astype(bf)
    c["identf"] = np.eye(128, dtype=np.float32)
    c["ones1"] = np.ones((1, 128), dtype=np.float32)
    c["ones1b"] = np.ones((1, 128), dtype=np.float32).astype(bf)
    c["o16c"] = np.full((128, 1), 1.0 / 16.0, dtype=np.float32)
    c["cbias"] = np.tile(np.array([[1e-5, 1.0]], dtype=np.float32), (128, 1))
    c["Lf"] = np.where(ar[:, None] <= ar[None, :], 1.0 / 16.0, 0.0).astype(np.float32)
    c["Lr"] = np.where(ar[:, None] > ar[None, :], 1.0 / 16.0, 0.0).astype(np.float32)
    cm = (ar[:, None] <= ar[None, :]).astype(np.float32)
    c["cm01x4"] = np.tile(cm, (1, 4)).astype(bf)
    a = ar[:, None]
    jc = np.arange(128)[None, :]
    mcur = (jc <= a) & (a - jc <= 64)
    j2 = np.arange(64)[None, :]
    mprev = j2 >= a
    mm_ = np.concatenate([np.where(mcur, 0.0, NEG),
                          np.where(mprev, 0.0, NEG)], axis=1).astype(np.float32)
    mf_ = mm_.copy()
    mf_[:, 128:] = NEG
    c["mmain"] = mm_.astype(bf)
    c["mfirst"] = mf_.astype(bf)
    pos = np.arange(N, dtype=np.float64)
    half = np.arange(64, dtype=np.float64)
    inv = 1.0 / (10000.0 ** (2.0 * half / 128.0))
    fr = pos[:, None] * inv[None, :]
    for nm, tab in (("cq", np.cos(fr)), ("sq", np.sin(fr)),
                    ("ck", np.cos(fr) / np.sqrt(128.0)),
                    ("sk", np.sin(fr) / np.sqrt(128.0))):
        t = tab.reshape(NT, 128, 64).transpose(1, 0, 2).reshape(128, NT * 64)
        hi = t.astype(np.float32)
        lo = (t - hi.astype(np.float64)).astype(np.float32)
        c[nm + "_hi"] = np.ascontiguousarray(hi)
        c[nm + "_lo"] = np.ascontiguousarray(lo)
    _CONST = c
    return c


class NS:
    pass


CONST_NAMES = ("identb", "identf", "ones1", "ones1b", "o16c", "cbias", "Lf",
               "Lr", "cm01x4", "mmain", "mfirst")
TAB_NAMES = ("cq", "sq", "ck", "sk")

DRAM_IN = {
    "x": ((N, D), F32),
    "qkvW": ((D, 6144), F32),
    "WbT": ((D, D), F32),
    "Wo": ((D, D), F32),
    "Arr": ((D, 128), F32),
    "Bblk": ((128, D), F32),
    "rT": ((D, 8), F32),
    "bb": ((1, D), F32),
    "identb": ((128, 128), BF16),
    "identf": ((128, 128), F32),
    "ones1": ((1, 128), F32),
    "ones1b": ((1, 128), BF16),
    "o16c": ((128, 1), F32),
    "cbias": ((128, 2), F32),
    "Lf": ((128, 128), F32),
    "Lr": ((128, 128), F32),
    "cm01x4": ((128, 512), BF16),
    "mmain": ((128, 192), BF16),
    "mfirst": ((128, 192), BF16),
    **{n + s: ((128, 1024), F32) for n in TAB_NAMES for s in ("_hi", "_lo")},
}
DRAM_TMP = {
    "ffnx": ((N, D), F32),
    "qb": ((N, D), F32),
    "kb": ((N, 1024), F32),
    "vb": ((N, 1024), F32),
    "ot": ((NT, 128, HQ * 128), BF16),
}
TABIO = [n + s for n in TAB_NAMES for s in ("_hi", "_lo")]
GROUP_IO = {
    "A": (["x", "qkvW", "WbT", "Arr", "Bblk", "rT", "bb"] + list(CONST_NAMES)
          + TABIO, [], ["ffnx", "qb", "kb", "vb"], []),
    "B": (list(CONST_NAMES) + TABIO, ["qb", "kb", "vb"], ["ot"], []),
    "C": (["Wo"], ["ffnx", "ot"], [], ["out"]),
    "F": (list(DRAM_IN.keys()), [], [], ["out"]),
}


def p_setup(em, h, t, B, tables=True):
    for nm in CONST_NAMES:
        em.dma("sync", getattr(t, nm)[:], getattr(h, nm)[:], writes=[B[nm]])
    if tables:
        for nm in TAB_NAMES:
            dst = getattr(t, nm)
            em.dma("sync", dst[:], getattr(h, nm + "_hi")[:], writes=[B[nm]])
            em.dma("sync", t.ttmp[:], getattr(h, nm + "_lo")[:], writes=[B["ttmp"]])
            em.op("vector", (lambda d: lambda E: E.tensor_add(d[:], d[:], t.ttmp[:]))(dst),
                  reads=[B[nm], B["ttmp"]], writes=[B[nm]], sem="mark")


def p1_ln(em, h, t, B):
    xnv = t.xn[:].rearrange("p (k n) -> p k n", k=NKC)
    pb = [t.pA, t.pB]
    pbB = [B["pA"], B["pB"]]
    st_ = t.p1_st
    Bst = B["p1_st"]
    for ti in range(NT):
        bi = ti % 2
        xt, xnt = t.p1_xt[bi], t.p1_xnt
        Bxt, Bxnt = B[f"p1_xt{bi}"], B["p1_xnt"]
        em.dma("sync", xt[:], h.x[ti * 128:(ti + 1) * 128, :], writes=[Bxt])
        for i8 in range(8):
            em.op("vector", (lambda x_, i_: lambda E: E.bn_stats(
                st_[:, 8 + 6 * i_:8 + 6 * (i_ + 1)],
                x_[:, 512 * i_:512 * (i_ + 1)]))(xt, i8),
                reads=[Bxt], writes=[Bst], sem="mark" if i8 == 7 else None)
        v6 = st_[:, 8:56].rearrange("p (g s) -> p s g", s=3)
        em.op("scalar", lambda E: E.activation(st_[:, 56:72], v6[:, 1, :], AF.Copy,
                                               accum_out=st_[:, 4:5]),
              reads=[Bst], writes=[Bst])
        em.op("scalar", lambda E: E.activation(st_[:, 56:72], v6[:, 1, :], AF.Square,
                                               accum_out=st_[:, 5:6]),
              reads=[Bst], writes=[Bst])
        em.op("scalar", lambda E: E.activation(st_[:, 56:72], v6[:, 2, :], AF.Copy,
                                               accum_out=st_[:, 3:4]),
              reads=[Bst], writes=[Bst])
        em.op("scalar", lambda E: E.activation(st_[:, 0:1], st_[:, 4:5], AF.Copy,
                                               scale=1.0 / 16),
              reads=[Bst], writes=[Bst])
        em.op("scalar", lambda E: E.activation(st_[:, 6:7], st_[:, 0:1], AF.Square),
              reads=[Bst], writes=[Bst], sem="mark")
        em.op("vector", lambda E: E.tensor_scalar(st_[:, 1:2], st_[:, 3:4], 1.0 / 4096,
                                                  None, ALU.mult),
              reads=[Bst], writes=[Bst])
        em.op("vector", lambda E: E.scalar_tensor_tensor(
            st_[:, 7:8], st_[:, 5:6], 1.0 / 16, st_[:, 1:2], ALU.mult, ALU.add),
              reads=[Bst], writes=[Bst])
        em.op("vector", lambda E: E.tensor_sub(st_[:, 1:2], st_[:, 7:8], st_[:, 6:7]),
              reads=[Bst], writes=[Bst], sem="mark")
        em.op("scalar", lambda E: E.activation(st_[:, 2:3], st_[:, 1:2], AF.Ln,
                                               bias=t.cbias[:, 0:1]),
              reads=[Bst, B["cbias"]], writes=[Bst])
        em.op("scalar", lambda E: E.activation(st_[:, 2:3], st_[:, 2:3], AF.Exp,
                                               scale=-0.5),
              reads=[Bst], writes=[Bst], sem="mark")
        em.op("vector", (lambda x_, o_: lambda E: E.tensor_scalar(
            o_[:], x_[:], st_[:, 0:1], st_[:, 2:3], ALU.subtract, ALU.mult))(xt, xnt),
            reads=[Bxt, Bst], writes=[Bxnt], sem="mark")
        for g in range(4):
            bank, Bb = pb[g % 2], pbB[g % 2]
            bv = bank[:].bitcast(BF16)
            for j in range(8):
                k = g * 8 + j
                em.op("tensor", TRS(bv[:, j * 128:(j + 1) * 128],
                                    xnt[:, k * 128:(k + 1) * 128], t.identb[:]),
                      reads=[Bxnt, B["identb"]], writes=[Bb],
                      sem="mark" if j == 7 else None)
            dst = xnv[:, g * 8:(g + 1) * 8, ti * 128:(ti + 1) * 128]
            em.op("vector", (lambda d_, s_: lambda E: E.tensor_copy(d_, s_[:, 0:1024]))(
                dst, bv), reads=[Bb], writes=[B["xn"]], sem="mark")



def gemm_kloop(em, B, banks, banksB, ring, Bring, lhs_ap, lhsB, src_w, fs, npass,
               th, n_k, last_stop):
    def mms(kk):
        tot2 = npass * n_k + kk
        for tt in range(8):
            ti = th * 8 + tt
            em.op("tensor", MM(banks[tt][:], lhs_ap(kk, ti),
                               ring[:, tot2 % 4, :], kk == 0,
                               (kk == n_k - 1) and last_stop),
                  reads=[lhsB, Bring[tot2 % 4]], writes=[banksB[tt]],
                  sem="mark" if tt == 7 else None)
    for k in range(n_k):
        tot = npass * n_k + k
        em.dma("gpsimd", ring[:, tot % 4, :],
               src_w[k * 128:(k + 1) * 128, fs * 512:(fs + 1) * 512],
               writes=[Bring[tot % 4]])
        if k >= 3:
            mms(k - 3)
    for kk in range(n_k - 3, n_k):
        mms(kk)


def p2_ffn(em, h, t, B):
    xnv = t.xn[:].rearrange("p (k n) -> p k n", k=NKC)
    em.dma("gpsimd", t.rTb[:].rearrange("p (k e) -> p k e", k=NKC),
           h.rT[:].rearrange("(k p) e -> p k e", p=128), writes=[B["rTb"]])
    em.dma("gpsimd", t.Arrb[:].rearrange("p (k e) -> p k e", k=NKC),
           h.Arr[:].rearrange("(k p) e -> p k e", p=128), writes=[B["Arrb"]])
    em.dma("gpsimd", t.Bblkb[:], h.Bblk[:], writes=[B["Bblkb"]])
    em.dma("gpsimd", t.bbb[:], h.bb[:], writes=[B["bbb"]])
    pR, pH, pT = t.pA, t.pB, t.pC
    BR, BH, BT = B["pA"], B["pB"], B["pC"]
    L, BL = t.p2_l, B["p2_l"]
    HS, BHS = t.p2_hs, B["p2_hs"]
    for ti in range(NT):
        for k in range(NKC):
            em.op("tensor", MM(pR[:, 0:8], xnv[:, k, ti * 128:(ti + 1) * 128],
                               t.rTb[:, k * 8:(k + 1) * 8], k == 0, k == NKC - 1),
                  reads=[B["xn"], B["rTb"]], writes=[BR],
                  sem="mark" if k == NKC - 1 else None)
        for k in range(NKC):
            em.op("tensor", MM(pH[:, 0:128], xnv[:, k, ti * 128:(ti + 1) * 128],
                               t.Arrb[:, k * 128:(k + 1) * 128], k == 0, k == NKC - 1),
                  reads=[B["xn"], B["Arrb"]], writes=[BH],
                  sem="mark" if k == NKC - 1 else None)
        em.op("vector", lambda E: E.tensor_copy(L[:, 0:8], pR[:, 0:8]),
              reads=[BR], writes=[BL], sem="mark")
        em.op("vector", lambda E: E.reduce_max(L[:, 8:9], L[:, 0:8], axis=AX.X,
                                               negate=True),
              reads=[BL], writes=[BL])
        em.op("vector", lambda E: E.tensor_scalar(L[:, 16:24], L[:, 0:8], L[:, 8:9],
                                                  0.0, ALU.add, ALU.is_equal),
              reads=[BL], writes=[BL])
        em.op("vector", lambda E: E.scalar_tensor_tensor(L[:, 24:32], L[:, 16:24],
                                                         NEG, L[:, 0:8], ALU.mult,
                                                         ALU.add),
              reads=[BL], writes=[BL])
        em.op("vector", lambda E: E.reduce_max(L[:, 9:10], L[:, 24:32], axis=AX.X,
                                               negate=True),
              reads=[BL], writes=[BL])
        em.op("vector", lambda E: E.tensor_scalar(L[:, 16:24], L[:, 0:8], L[:, 9:10],
                                                  0.0, ALU.add, ALU.is_ge),
              reads=[BL], writes=[BL], sem="mark")
        em.op("scalar", lambda E: E.activation(L[:, 32:40], L[:, 0:8], AF.Exp,
                                               bias=L[:, 8:9]),
              reads=[BL], writes=[BL], sem="mark")
        em.op("vector", lambda E: E.tensor_mul(L[:, 32:40], L[:, 32:40], L[:, 16:24]),
              reads=[BL], writes=[BL])
        em.op("vector", lambda E: E.reduce_sum(L[:, 10:11], L[:, 32:40], axis=AX.X),
              reads=[BL], writes=[BL])
        em.op("vector", lambda E: E.reciprocal(L[:, 11:12], L[:, 10:11]),
              reads=[BL], writes=[BL])
        em.op("vector", lambda E: E.tensor_scalar(L[:, 32:40], L[:, 32:40],
                                                  L[:, 11:12], 2.0, ALU.mult, ALU.mult),
              reads=[BL], writes=[BL], sem="mark")
        for e in range(8):
            em.op("vector", (lambda e_: lambda E: E.tensor_scalar(
                HS[:, e_ * 16:(e_ + 1) * 16], pH[:, e_ * 16:(e_ + 1) * 16],
                L[:, 32 + e_:33 + e_], None, ALU.mult))(e),
                reads=[BH, BL], writes=[BHS], sem="mark" if e == 7 else None)
        em.op("tensor", TRS(pT[:].bitcast(BF16)[:, 0:128], HS[:], t.identb[:]),
              reads=[BHS, B["identb"]], writes=[BT], sem="mark")
        em.op("vector", (lambda ti_: lambda E: E.tensor_copy(
            t.hst[:, ti_ * 128:(ti_ + 1) * 128], pT[:].bitcast(BF16)[:, 0:128]))(ti),
            reads=[BT], writes=[B["hst"]], sem="mark")
    em.barrier("p2a")

    banks, banksB = t.pG, [B[f"pG{i}"] for i in range(8)]
    ring, Bring = t.wring, [B[f"wring{i}"] for i in range(4)]
    xs, Bxs = t.xs, [B[f"xs{i}"] for i in range(8)]
    npass = 0
    for fs in range(8):
        for th in range(2):
            for tt in range(8):
                ti = th * 8 + tt
                em.dma("sync", xs[:, tt, :],
                       h.x[ti * 128:(ti + 1) * 128, fs * 512:(fs + 1) * 512],
                       writes=[Bxs[tt]])
            gemm_kloop(em, B, banks, banksB, ring, Bring,
                       (lambda kk, ti: xnv[:, kk, ti * 128:(ti + 1) * 128]),
                       B["xn"], h.WbT, fs, npass, th, NKC, False)
            for tt in range(8):
                ti = th * 8 + tt
                em.op("tensor", MM(banks[tt][:], t.hst[:, ti * 128:(ti + 1) * 128],
                                   t.Bblkb[:, fs * 512:(fs + 1) * 512], False, False),
                      reads=[B["hst"], B["Bblkb"]], writes=[banksB[tt]])
                em.op("tensor", MM(banks[tt][:], t.ones1b[:],
                                   t.bbb[0:1, fs * 512:(fs + 1) * 512], False, True),
                      reads=[B["ones1b"], B["bbb"]], writes=[banksB[tt]], sem="mark")
            for tt in range(8):
                ti = th * 8 + tt
                ob, Bob = t.outb, B[f"outb{tt % 4}"]
                em.op("vector", (lambda b_, tt_: lambda E: E.tensor_add(
                    ob[:, tt_ % 4, :], b_[:], xs[:, tt_, :]))(banks[tt], tt),
                    reads=[banksB[tt], Bxs[tt]], writes=[Bob], sem="mark")
                em.dma("sync", h.ffnx[ti * 128:(ti + 1) * 128,
                                      fs * 512:(fs + 1) * 512],
                       ob[:, tt % 4, :], reads=[Bob])
            npass += 1


def p3_qkv(em, h, t, B):
    xnv = t.xn[:].rearrange("p (k n) -> p k n", k=NKC)
    banks, banksB = t.pG, [B[f"pG{i}"] for i in range(8)]
    ring, Bring = t.wring, [B[f"wring{i}"] for i in range(4)]
    npass = 0
    for fs in range(12):
        for th in range(2):
            gemm_kloop(em, B, banks, banksB, ring, Bring,
                       (lambda kk, ti: xnv[:, kk, ti * 128:(ti + 1) * 128]),
                       B["xn"], h.qkvW, fs, npass, th, NKC, True)
            if fs < 8:
                dst, col = h.qb, fs * 512
            elif fs < 10:
                dst, col = h.kb, (fs - 8) * 512
            else:
                dst, col = h.vb, (fs - 10) * 512
            for tt in range(8):
                ti = th * 8 + tt
                ob, Bob = t.outb, B[f"outb{tt % 4}"]
                em.op("vector", (lambda b_, tt_: lambda E: E.tensor_copy(
                    ob[:, tt_ % 4, :], b_[:]))(banks[tt], tt),
                    reads=[banksB[tt]], writes=[Bob], sem="mark")
                em.dma("sync", dst[ti * 128:(ti + 1) * 128, col:col + 512],
                       ob[:, tt % 4, :], reads=[Bob])
            npass += 1


def p4_attn(em, h, t, B):
    idb, idf = t.identb, t.identf
    pS, BS = t.pS, B["pS"]
    pW = [t.pA, t.pB]
    BW = [B["pA"], B["pB"]]
    pSC = [t.pC, t.pD]
    BSC = [B["pC"], B["pD"]]
    pAT = [t.pE, t.pF]
    BAT = [B["pE"], B["pF"]]
    KF, VF, QF = t.p4_kf, t.p4_vf, t.p4_qf
    BKF, BVF, BQF = B["p4_kf"], B["p4_vf"], B["p4_qf"]
    S1, S2 = t.p4_s1, t.p4_s2
    BS1, BS2 = B["p4_s1"], B["p4_s2"]
    G, BG = t.p4_g, B["p4_g"]
    CN, BCN = t.p4_cneg, B["p4_cneg"]
    EV, BEV = t.p4_erev, B["p4_erev"]

    for c in range(NT):
        kb_ = c % 2
        em.dma("sync", KF[:], h.kb[c * 128:(c + 1) * 128, :], writes=[BKF])
        em.dma("sync", VF[:], h.vb[c * 128:(c + 1) * 128, :], writes=[BVF])
        em.dma("sync", QF[:], h.qb[c * 128:(c + 1) * 128, :], writes=[BQF])

        em.op("scalar", lambda E: E.activation(t.p4_kh[:], KF[:], AF.Exp),
              reads=[BKF], writes=[B["p4_kh"]], sem="mark")
        em.op("vector", lambda E: E.reduce_sum(
            t.p4_zk[:], t.p4_kh[:].rearrange("p (g n) -> p g n", g=8), axis=AX.X),
            reads=[B["p4_kh"]], writes=[B["p4_zk"]])
        em.op("vector", lambda E: E.reciprocal(t.p4_rv[:], t.p4_zk[:]),
              reads=[B["p4_zk"]], writes=[B["p4_rv"]], sem="mark")
        em.op("vector", lambda E: E.tensor_tensor(
            t.p4_vpb[:].rearrange("p (g n) -> p g n", g=8),
            VF[:].rearrange("p (g n) -> p g n", g=8),
            t.p4_rv[:, :, None].broadcast_to((128, 8, 128)), ALU.mult),
            reads=[BVF, B["p4_rv"]], writes=[B["p4_vpb"]], sem="mark")
        em.op("gpsimd", (lambda b: lambda E: E.tensor_copy(t.p4_vb[b][:], VF[:]))(kb_),
              reads=[BVF], writes=[B[f"p4_vb{kb_}"]], sem="mark")
        em.op("scalar", lambda E: E.activation(t.p4_ksa[:], KF[:], AF.Abs),
              reads=[BKF], writes=[B["p4_ksa"]])
        em.op("scalar", lambda E: E.activation(t.p4_ksb[:], t.p4_ksa[:], AF.Exp,
                                               scale=-1.0),
              reads=[B["p4_ksa"]], writes=[B["p4_ksb"]])
        em.op("scalar", lambda E: E.activation(t.p4_ksa[:], t.p4_ksb[:], AF.Ln,
                                               bias=t.cbias[:, 1:2]),
              reads=[B["p4_ksb"], B["cbias"]], writes=[B["p4_ksa"]], sem="mark")
        em.op("vector", lambda E: E.tensor_scalar_min(t.p4_ksb[:], KF[:], 0.0),
              reads=[BKF], writes=[B["p4_ksb"]])
        em.op("vector", lambda E: E.tensor_sub(G[:], t.p4_ksb[:], t.p4_ksa[:]),
              reads=[B["p4_ksb"], B["p4_ksa"]], writes=[BG], sem="mark")
        for hh in range(8):
            em.op("tensor", MM(pW[0][:, hh:hh + 1], G[:, hh * 128:(hh + 1) * 128],
                               t.o16c[:], True, True),
                  reads=[BG, B["o16c"]], writes=[BW[0]],
                  sem="mark" if hh == 7 else None)
        em.op("scalar", lambda E: E.activation(t.p4_dec[:, 0:8], pW[0][:, 0:8], AF.Exp),
              reads=[BW[0]], writes=[B["p4_dec"]])
        em.op("scalar", lambda E: E.activation(t.p4_dec[:, 8:16], pW[0][:, 0:8],
                                               AF.Exp, scale=0.5),
              reads=[BW[0]], writes=[B["p4_dec"]], sem="mark")
        for half in range(2):
            em.op("tensor", MM(pW[1][0:1, 0:512], t.o16c[:],
                               G[:, half * 512:(half + 1) * 512], True, True),
                  reads=[BG, B["o16c"]], writes=[BW[1]], sem="mark")
            em.op("scalar", (lambda hf: lambda E: E.activation(
                CN[0:1, hf * 512:(hf + 1) * 512], pW[1][0:1, 0:512], AF.Copy,
                scale=-0.5))(half),
                reads=[BW[1]], writes=[BCN], sem="mark")
        for half in range(2):
            bank, Bb = pW[half], BW[half]
            em.op("tensor", MM(bank[:, 0:512], t.Lf[:],
                               G[:, half * 512:(half + 1) * 512], True, False),
                  reads=[BG, B["Lf"]], writes=[Bb])
            em.op("tensor", MM(bank[:, 0:512], t.ones1[:],
                               CN[0:1, half * 512:(half + 1) * 512], False, True),
                  reads=[BCN, B["ones1"]], writes=[Bb], sem="mark")
            em.op("scalar", (lambda hf, bk: lambda E: E.activation(
                t.p4_ea[:, hf * 512:(hf + 1) * 512], bk[:, 0:512], AF.Exp))(half, bank),
                reads=[Bb], writes=[B["p4_ea"]], sem="mark")
        for half in range(2):
            bank, Bb = pW[half], BW[half]
            em.op("tensor", MM(bank[:, 0:512], t.Lr[:],
                               G[:, half * 512:(half + 1) * 512], True, True),
                  reads=[BG, B["Lr"]], writes=[Bb], sem="mark")
            em.op("scalar", (lambda hf, bk: lambda E: E.activation(
                EV[:, hf * 512:(hf + 1) * 512], bk[:, 0:512], AF.Exp))(half, bank),
                reads=[Bb], writes=[BEV], sem="mark")
        em.op("vector", lambda E: E.tensor_mul(t.p4_khg[:], t.p4_kh[:], EV[:]),
              reads=[B["p4_kh"], BEV], writes=[B["p4_khg"]], sem="mark")
        for half in range(2):
            bank, Bb = pW[half], BW[half]
            em.op("tensor", MM(bank[:, 0:512], t.ones1[:],
                               CN[0:1, half * 512:(half + 1) * 512], False, True),
                  reads=[BCN, B["ones1"]], writes=[Bb], sem="mark")
            em.op("scalar", (lambda hf, bk: lambda E: E.activation(
                EV[:, hf * 512:(hf + 1) * 512], bk[:, 0:512], AF.Exp))(half, bank),
                reads=[Bb], writes=[BEV], sem="mark")
        em.op("vector", lambda E: E.tensor_mul(t.p4_kt1[:], t.p4_kh[:], EV[:]),
              reads=[B["p4_kh"], BEV], writes=[B["p4_kt1"]], sem="mark")
        kf3 = KF[:].rearrange("p (g two n) -> p g two n", g=8, two=2)
        ksa3 = t.p4_ksa[:].rearrange("p (g two n) -> p g two n", g=8, two=2)
        em.op("gpsimd", lambda E: E.tensor_scalar_mul(ksa3[:, :, 0:1, :],
                                                    kf3[:, :, 1:2, :], -1.0),
              reads=[BKF], writes=[B["p4_ksa"]])
        em.op("gpsimd", lambda E: E.tensor_copy(ksa3[:, :, 1:2, :], kf3[:, :, 0:1, :]),
              reads=[BKF], writes=[B["p4_ksa"]], sem="mark")
        ckb = t.ck[:, c * 64:(c + 1) * 64][:, None, None, :].broadcast_to((128, 8, 2, 64))
        skb = t.sk[:, c * 64:(c + 1) * 64][:, None, None, :].broadcast_to((128, 8, 2, 64))
        em.op("vector", (lambda cb: lambda E: E.tensor_tensor(
            t.p4_ksb[:].rearrange("p (g two n) -> p g two n", g=8, two=2), kf3, cb,
            ALU.mult))(ckb),
            reads=[BKF, B["ck"]], writes=[B["p4_ksb"]], sem="mark")
        em.op("gpsimd", (lambda sb2: lambda E: E.tensor_tensor(ksa3, ksa3, sb2,
                                                             ALU.mult))(skb),
              reads=[B["p4_ksa"], B["sk"]], writes=[B["p4_ksa"]], sem="mark")
        em.op("vector", lambda E: E.tensor_add(t.p4_skb[:], t.p4_ksb[:], t.p4_ksa[:]),
              reads=[B["p4_ksb"], B["p4_ksa"]], writes=[B["p4_skb"]], sem="mark")
        p0v = pSC[0][:].bitcast(BF16)
        p1v = pSC[1][:].bitcast(BF16)
        for hh in range(8):
            em.op("tensor", TRS(p0v[:, hh * 128:(hh + 1) * 128],
                                t.p4_kt1[:, hh * 128:(hh + 1) * 128], idb[:]),
                  reads=[B["p4_kt1"], B["identb"]], writes=[BSC[0]],
                  sem="mark" if hh == 7 else None)
        em.op("vector", lambda E: E.tensor_copy(t.p4_ktt[:], p0v[:, 0:1024]),
              reads=[BSC[0]], writes=[B["p4_ktt"]], sem="mark")
        for hh in range(8):
            em.op("tensor", TRS(p1v[:, hh * 128:(hh + 1) * 128],
                                t.p4_skb[:, hh * 128:(hh + 1) * 128], idb[:]),
                  reads=[B["p4_skb"], B["identb"]], writes=[BSC[1]],
                  sem="mark" if hh == 7 else None)
        em.op("scalar", (lambda b: lambda E: E.activation(
            t.p4_skt[b][:], p1v[:, 0:1024], AF.Copy))(kb_),
            reads=[BSC[1]], writes=[B[f"p4_skt{kb_}"]], sem="mark")
        if c >= 1:
            for hh in range(8):
                em.op("scalar", (lambda h_: lambda E: E.activation(
                    t.p4_sopb[:, h_ * 128:(h_ + 1) * 128],
                    pS[:, h_ * 128:(h_ + 1) * 128], AF.Copy,
                    scale=t.p4_dec[:, 8 + h_:9 + h_]))(hh),
                    reads=[BS, B["p4_dec"]], writes=[B["p4_sopb"]],
                    sem="mark" if hh == 7 else None)
        if c < NT - 1:
            if c >= 1:
                em.op("vector", lambda E: E.tensor_tensor(
                    pS[:].rearrange("p (g n) -> p g n", g=8),
                    pS[:].rearrange("p (g n) -> p g n", g=8),
                    t.p4_dec[:, 0:8, None].broadcast_to((128, 8, 128)), ALU.mult),
                    reads=[BS, B["p4_dec"]], writes=[BS], sem="mark")
            for hh in range(8):
                em.op("tensor", MM(pS[:, hh * 128:(hh + 1) * 128],
                                   t.p4_khg[:, hh * 128:(hh + 1) * 128],
                                   t.p4_vpb[:, hh * 128:(hh + 1) * 128],
                                   c == 0 and hh % 4 == 0, c == NT - 2),
                      reads=[B["p4_khg"], B["p4_vpb"]], writes=[BS],
                      sem="mark" if hh == 7 else None)

        qf3 = QF[:].rearrange("p (g two n) -> p g two n", g=32, two=2)
        s13 = S1[:].rearrange("p (g two n) -> p g two n", g=32, two=2)
        em.op("gpsimd", lambda E: E.tensor_scalar_mul(s13[:, :, 0:1, :],
                                                    qf3[:, :, 1:2, :], -1.0),
              reads=[BQF], writes=[BS1])
        em.op("gpsimd", lambda E: E.tensor_copy(s13[:, :, 1:2, :], qf3[:, :, 0:1, :]),
              reads=[BQF], writes=[BS1], sem="mark")
        cqb = t.cq[:, c * 64:(c + 1) * 64][:, None, None, :].broadcast_to((128, 32, 2, 64))
        sqb = t.sq[:, c * 64:(c + 1) * 64][:, None, None, :].broadcast_to((128, 32, 2, 64))
        em.op("vector", (lambda cb: lambda E: E.tensor_tensor(
            S2[:].rearrange("p (g two n) -> p g two n", g=32, two=2), qf3, cb,
            ALU.mult))(cqb),
            reads=[BQF, B["cq"]], writes=[BS2], sem="mark")
        em.op("gpsimd", (lambda sb2: lambda E: E.tensor_tensor(s13, s13, sb2,
                                                             ALU.mult))(sqb),
              reads=[BS1, B["sq"]], writes=[BS1], sem="mark")
        em.op("vector", lambda E: E.tensor_add(S1[:], S1[:], S2[:]),
              reads=[BS1, BS2], writes=[BS1], sem="mark")
        for g in range(8):
            bank, Bb = pSC[g % 2], BSC[g % 2]
            for j in range(4):
                hh = g * 4 + j
                em.op("tensor", TRS(bank[:, j * 128:(j + 1) * 128],
                                    S1[:, hh * 128:(hh + 1) * 128], idf[:]),
                      reads=[BS1, B["identf"]], writes=[Bb],
                      sem="mark" if j == 3 else None)
            eng = "vector" if g % 2 == 0 else "scalar"
            if eng == "vector":
                em.op("vector", (lambda g_, b_: lambda E: E.tensor_copy(
                    t.p4_sqt[:, g_ * 512:(g_ + 1) * 512], b_[:]))(g, bank),
                    reads=[Bb], writes=[B["p4_sqt"]], sem="mark")
            else:
                em.op("scalar", (lambda g_, b_: lambda E: E.activation(
                    t.p4_sqt[:, g_ * 512:(g_ + 1) * 512], b_[:], AF.Copy))(g, bank),
                    reads=[Bb], writes=[B["p4_sqt"]], sem="mark")
        em.op("scalar", lambda E: E.activation(S2[:], QF[:], AF.Exp),
              reads=[BQF], writes=[BS2], sem="mark")
        em.op("vector", lambda E: E.reduce_sum(
            t.p4_zq[:], S2[:].rearrange("p (g n) -> p g n", g=32), axis=AX.X),
            reads=[BS2], writes=[B["p4_zq"]])
        em.op("vector", lambda E: E.reciprocal(t.p4_rq[:], t.p4_zq[:]),
              reads=[B["p4_zq"]], writes=[B["p4_rq"]], sem="mark")
        ea4 = t.p4_ea[:].rearrange("p (g n) -> p g n", g=8)[:, :, None, :].broadcast_to(
            (128, 8, 4, 128))
        em.op("vector", lambda E: E.tensor_tensor(
            S2[:].rearrange("p (g q n) -> p g q n", g=8, q=4),
            S2[:].rearrange("p (g q n) -> p g q n", g=8, q=4), ea4, ALU.mult),
            reads=[BS2, B["p4_ea"]], writes=[BS2], sem="mark")
        em.op("vector", lambda E: E.tensor_tensor(
            S2[:].rearrange("p (g n) -> p g n", g=32),
            S2[:].rearrange("p (g n) -> p g n", g=32),
            t.p4_rq[:, :, None].broadcast_to((128, 32, 128)), ALU.mult),
            reads=[BS2, B["p4_rq"]], writes=[BS2], sem="mark")
        for g in range(8):
            bank, Bb = pSC[g % 2], BSC[g % 2]
            for j in range(4):
                hh = g * 4 + j
                em.op("tensor", TRS(bank[:, j * 128:(j + 1) * 128],
                                    S2[:, hh * 128:(hh + 1) * 128], idf[:]),
                      reads=[BS2, B["identf"]], writes=[Bb],
                      sem="mark" if j == 3 else None)
            eng = "vector" if g % 2 == 0 else "scalar"
            if eng == "vector":
                em.op("vector", (lambda g_, b_: lambda E: E.tensor_copy(
                    t.p4_qtt[:, g_ * 512:(g_ + 1) * 512], b_[:]))(g, bank),
                    reads=[Bb], writes=[B["p4_qtt"]], sem="mark")
            else:
                em.op("scalar", (lambda g_, b_: lambda E: E.activation(
                    t.p4_qtt[:, g_ * 512:(g_ + 1) * 512], b_[:], AF.Copy))(g, bank),
                    reads=[Bb], writes=[B["p4_qtt"]], sem="mark")
        SKTc, SKTp = t.p4_skt[kb_], t.p4_skt[1 - kb_]
        BSKTc, BSKTp = B[f"p4_skt{kb_}"], B[f"p4_skt{1 - kb_}"]
        VBc, VBp = t.p4_vb[kb_], t.p4_vb[1 - kb_]
        BVBc, BVBp = B[f"p4_vb{kb_}"], B[f"p4_vb{1 - kb_}"]
        for g in range(8):
            em.op("tensor", MM(pAT[g % 2][:], t.p4_ktt[:, g * 128:(g + 1) * 128],
                               t.p4_qtt[:, g * 512:(g + 1) * 512], True, True),
                  reads=[B["p4_ktt"], B["p4_qtt"]], writes=[BAT[g % 2]], sem="mark")
            em.op("vector", (lambda g_: lambda E: E.tensor_tensor(
                t.p4_atm[:, (g_ % 2) * 512:(g_ % 2) * 512 + 512], pAT[g_ % 2][:],
                t.cm01x4[:], ALU.mult))(g),
                reads=[BAT[g % 2], B["cm01x4"]], writes=[B[f"p4_atm{g % 2}"]],
                sem="mark")
            ob, Bob = pW[g % 2], BW[g % 2]
            for j in range(4):
                hh = g * 4 + j
                slot = ob[:, j * 128:(j + 1) * 128]
                if c >= 1:
                    em.op("tensor", MM(slot, t.p4_sopb[:, g * 128:(g + 1) * 128],
                                       t.p4_qtt[:, hh * 128:(hh + 1) * 128],
                                       j == 0, False),
                          reads=[B["p4_sopb"], B["p4_qtt"]], writes=[Bob])
                em.op("tensor", MM(slot, t.p4_vpb[:, g * 128:(g + 1) * 128],
                                   t.p4_atm[:, (g % 2) * 512 + j * 128:
                                            (g % 2) * 512 + (j + 1) * 128],
                                   c == 0 and j == 0, False),
                      reads=[B["p4_vpb"], B[f"p4_atm{g % 2}"]], writes=[Bob])
                sb_, Bsb = pSC[(hh // 2) % 2], BSC[(hh // 2) % 2]
                sl = hh % 2
                em.op("tensor", MM(sb_[:, sl * 192:sl * 192 + 128],
                                   t.p4_sqt[:, hh * 128:(hh + 1) * 128],
                                   SKTc[:, g * 128:(g + 1) * 128], sl == 0, False),
                      reads=[B["p4_sqt"], BSKTc], writes=[Bsb])
                if c >= 1:
                    em.op("tensor", MM(sb_[:, sl * 192 + 128:sl * 192 + 192],
                                       t.p4_sqt[:, hh * 128:(hh + 1) * 128],
                                       SKTp[:, g * 128 + 64:(g + 1) * 128],
                                       False, False),
                          reads=[B["p4_sqt"], BSKTp], writes=[Bsb])
                em.op("tensor", MM(sb_[:, sl * 192:sl * 192 + 192], idb[:],
                                   t.mfirst[:] if c == 0 else t.mmain[:], False,
                                   sl == 1),
                      reads=[B["identb"], B["mfirst" if c == 0 else "mmain"]],
                      writes=[Bsb], sem="mark")
                if hh % 2 == 1:
                    for h2 in (hh - 1, hh):
                        s2l = h2 % 2
                        em.op("scalar", (lambda sbx, s2_: lambda E: E.activation(
                            t.p4_pb[:, s2_ * 192:(s2_ + 1) * 192],
                            sbx[:, s2_ * 192:s2_ * 192 + 192], AF.Exp,
                            accum_out=t.p4_zs[:, s2_:s2_ + 1]))(sb_, s2l),
                            reads=[Bsb], writes=[B["p4_pb"], B["p4_zs"]],
                            sem="mark" if h2 == hh else None)
                    em.op("vector", lambda E: E.reciprocal(t.p4_rs[:, 0:2],
                                                           t.p4_zs[:, 0:2]),
                          reads=[B["p4_zs"]], writes=[B["p4_rs"]])
                    for h2 in (hh - 1, hh):
                        s2l = h2 % 2
                        em.op("vector", (lambda s2_: lambda E: E.tensor_scalar(
                            t.p4_pb[:, s2_ * 192:(s2_ + 1) * 192],
                            t.p4_pb[:, s2_ * 192:(s2_ + 1) * 192],
                            t.p4_rs[:, s2_:s2_ + 1], None, ALU.mult))(s2l),
                            reads=[B["p4_rs"], B["p4_pb"]], writes=[B["p4_pb"]],
                            sem="mark" if h2 == hh else None)
                    bv2 = sb_[:].bitcast(BF16)
                    for h2 in (hh - 1, hh):
                        s2l = h2 % 2
                        off = s2l * 256
                        em.op("tensor", TRS(bv2[:, off:off + 128],
                                            t.p4_pb[:, s2l * 192:s2l * 192 + 128],
                                            idb[:]),
                              reads=[B["p4_pb"], B["identb"]], writes=[Bsb])
                        em.op("tensor", TRS(bv2[64:128, off + 128:off + 256],
                                            t.p4_pb[:, s2l * 192 + 128:
                                                    s2l * 192 + 192], idb[:]),
                              reads=[B["p4_pb"], B["identb"]], writes=[Bsb],
                              sem="mark" if h2 == hh else None)
                    em.op("vector", (lambda b_: lambda E: E.tensor_copy(
                        t.p4_ptb[:, 0:512], b_[:, 0:512]))(bv2),
                        reads=[Bsb], writes=[B["p4_ptb"]], sem="mark")
                    for h2 in (hh - 1, hh):
                        j2 = h2 % 4
                        slot2 = ob[:, j2 * 128:(j2 + 1) * 128]
                        off = (h2 % 2) * 256
                        em.op("tensor", MM(slot2, VBc[:, g * 128:(g + 1) * 128],
                                           t.p4_ptb[:, off:off + 128], False,
                                           c == 0),
                              reads=[BVBc, B["p4_ptb"]], writes=[Bob],
                              sem="mark" if (c == 0 and h2 == hh) else None)
                        if c >= 1:
                            em.op("tensor", MM(slot2,
                                               VBp[64:128, g * 128:(g + 1) * 128],
                                               t.p4_ptb[64:128, off + 128:off + 256],
                                               False, True),
                                  reads=[BVBp, B["p4_ptb"]], writes=[Bob],
                                  sem="mark" if h2 == hh else None)
            em.op("scalar", (lambda g_, ob_: lambda E: E.activation(
                t.p4_ots[:, g_ * 512:(g_ + 1) * 512], ob_[:], AF.Copy, scale=0.5))(
                g, ob),
                reads=[Bob], writes=[B["p4_ots"]], sem="mark")
        em.dma("sync", h.ot[c, :, :], t.p4_ots[:], reads=[B["p4_ots"]])


def p5_out(em, h, t, B):
    otv = t.otr[:].rearrange("p (g n) -> p g n", g=HQ)
    for c in range(NT):
        em.dma("sync", otv[:, :, c * 128:(c + 1) * 128],
               h.ot[c, :, :].rearrange("p (g n) -> p g n", g=HQ), writes=[B["otr"]])
    banks, banksB = t.pG, [B[f"pG{i}"] for i in range(8)]
    ring, Bring = t.wring, [B[f"wring{i}"] for i in range(4)]
    fx, Bfx = t.xs, [B[f"xs{i}"] for i in range(8)]
    npass = 0
    for fs in range(8):
        for th in range(2):
            for tt in range(8):
                ti = th * 8 + tt
                em.dma("sync", fx[:, tt, :],
                       h.ffnx[ti * 128:(ti + 1) * 128, fs * 512:(fs + 1) * 512],
                       writes=[Bfx[tt]])
            gemm_kloop(em, B, banks, banksB, ring, Bring,
                       (lambda kk, ti: otv[:, kk, ti * 128:(ti + 1) * 128]),
                       B["otr"], h.Wo, fs, npass, th, NKC, True)
            for tt in range(8):
                ti = th * 8 + tt
                ob, Bob = t.outb, B[f"outb{tt % 4}"]
                em.op("vector", (lambda b_, tt_: lambda E: E.tensor_add(
                    ob[:, tt_ % 4, :], b_[:], fx[:, tt_, :]))(banks[tt], tt),
                    reads=[banksB[tt], Bfx[tt]], writes=[Bob], sem="mark")
                em.dma("sync", h.out[ti * 128:(ti + 1) * 128, fs * 512:(fs + 1) * 512],
                       ob[:, tt % 4, :], reads=[Bob])
            npass += 1


def build_group(group):
    nc = bass.Bass("TRN2", target_bir_lowering=False)
    es = ExitStack()
    em = Em(nc, es)
    ins, tmp_r, tmp_w, outs = GROUP_IO[group]
    h = NS()
    for nm in ins:
        shape, dt = DRAM_IN[nm]
        setattr(h, nm, nc.dram_tensor(nm, list(shape), dt, kind="ExternalInput").ap())
    for nm in tmp_r:
        shape, dt = DRAM_TMP[nm]
        setattr(h, nm, nc.dram_tensor(nm, list(shape), dt, kind="ExternalInput").ap())
    for nm in tmp_w:
        shape, dt = DRAM_TMP[nm]
        kind = "Internal" if group == "F" else "ExternalOutput"
        setattr(h, nm, nc.dram_tensor(nm, list(shape), dt, kind=kind).ap())
    if group == "F":
        for nm in DRAM_TMP:
            if not hasattr(h, nm):
                shape, dt = DRAM_TMP[nm]
                setattr(h, nm, nc.dram_tensor(nm, list(shape), dt, kind="Internal").ap())
    for nm in outs:
        setattr(h, nm, nc.dram_tensor(nm, [N, D], F32, kind="ExternalOutput").ap())

    t = NS()
    B = {}

    def sb(name, shape, dtype, stk=es):
        x = stk.enter_context(nc.sbuf_tensor(name, shape, dtype))
        B[name] = Buf(name)
        return x

    def ps(name, shape, dtype=F32, stk=es):
        x = stk.enter_context(nc.psum_tensor(name, shape, dtype))
        B[name] = Buf(name)
        return x

    for nm in CONST_NAMES:
        shape, dt = DRAM_IN[nm]
        x = es.enter_context(nc.sbuf_tensor("c_" + nm, list(shape), dt))
        B[nm] = Buf(nm)
        setattr(t, nm, x)
    if group in ("B", "F"):
        for nm in TAB_NAMES:
            x = es.enter_context(nc.sbuf_tensor("c_" + nm, [128, 1024], F32))
            B[nm] = Buf(nm)
            setattr(t, nm, x)
        t.ttmp = None
    t.wring = sb("wring_", [128, 4, 512], BF16)
    for i in range(4):
        B[f"wring{i}"] = Buf(f"wring{i}")
    t.outb = sb("outb_", [128, 4, 512], F32)
    for i in range(4):
        B[f"outb{i}"] = Buf(f"outb{i}")

    t.ttmp = getattr(t, "ttmp", None)
    if group in ("A", "F"):
        es_xn = ExitStack()
        t.xn = sb("xn", [128, NKC * N], BF16, es_xn)
        t.pA = ps("pA", [128, 512])
        t.pB = ps("pB", [128, 512])
        t.pC = ps("pC", [128, 512])
        es1 = ExitStack()
        t.p1_xt = [sb(f"p1_xt{i}", [128, 4096], F32, es1) for i in range(2)]
        t.p1_xnt = sb("p1_xnt", [128, 4096], BF16, es1)
        t.p1_st = sb("p1_st", [128, 72], F32, es1)
        if t.ttmp is None and group == "F":
            t.ttmp = sb("ttmp", [128, 1024], F32, es1)
        if DEBUG_DUMP:
            h.xnd = nc.dram_tensor("xnd", [128, NKC * N], BF16,
                                   kind="ExternalOutput").ap()
            h.hstd = nc.dram_tensor("hstd", [128, NT * 128], BF16,
                                    kind="ExternalOutput").ap()
            h.ld = nc.dram_tensor("ld", [128, 48], F32,
                                  kind="ExternalOutput").ap()
        p_setup(em, h, t, B, tables=(group == "F"))
        em.barrier("setup_end")
        p1_ln(em, h, t, B)
        em.barrier("p1_end")
        if DEBUG_DUMP:
            em.dma("sync", h.xnd[:], t.xn[:], reads=[B["xn"]])
        es1.close()
        es2 = ExitStack()
        t.rTb = sb("rTb", [128, NKC * 8], BF16, es2)
        t.Arrb = sb("Arrb", [128, NKC * 128], BF16, es2)
        t.Bblkb = sb("Bblkb", [128, 4096], BF16, es2)
        t.bbb = sb("bbb", [1, 4096], BF16, es2)
        t.hst = sb("hst", [128, NT * 128], BF16, es2)
        t.p2_l = sb("p2_l", [128, 48], F32, es2)
        t.p2_hs = sb("p2_hs", [128, 128], BF16, es2)
        t.xs = sb("xs_", [128, 8, 512], F32, es2)
        for i in range(8):
            B[f"xs{i}"] = Buf(f"xs{i}")
        esg = ExitStack()
        pgx = [ps(f"pGx{i}", [128, 512], F32, esg) for i in range(5)]
        t.pG = [t.pA, t.pB, t.pC] + pgx
        for i in range(8):
            B[f"pG{i}"] = B[["pA", "pB", "pC", "pGx0", "pGx1", "pGx2", "pGx3",
                             "pGx4"][i]]
        p2_ffn(em, h, t, B)
        em.barrier("p2_end")
        if DEBUG_DUMP:
            em.dma("sync", h.hstd[:], t.hst[:], reads=[B["hst"]])
            em.dma("sync", h.ld[:], t.p2_l[:], reads=[B["p2_l"]])
        p3_qkv(em, h, t, B)
        em.barrier("p3_end")
        es2.close()
        esg.close()
        es_xn.close()

    if group in ("B", "F"):
        es4 = ExitStack()

        def sb4(name, shape, dtype):
            return sb(name, shape, dtype, es4)

        if getattr(t, "ttmp", None) is None:
            t.ttmp = sb4("ttmp", [128, 1024], F32)
        t.p4_kf = sb4("p4_kf", [128, 1024], F32)
        t.p4_vf = sb4("p4_vf", [128, 1024], F32)
        t.p4_qf = sb4("p4_qf", [128, 4096], F32)
        t.p4_kh = sb4("p4_kh", [128, 1024], F32)
        t.p4_ksa = sb4("p4_ksa", [128, 1024], F32)
        t.p4_ksb = sb4("p4_ksb", [128, 1024], F32)
        t.p4_g = sb4("p4_g", [128, 1024], F32)
        t.p4_ea = sb4("p4_ea", [128, 1024], F32)
        t.p4_erev = sb4("p4_erev", [128, 1024], F32)
        t.p4_cneg = sb4("p4_cneg", [1, 1024], F32)
        t.p4_dec = sb4("p4_dec", [128, 16], F32)
        t.p4_kt1 = sb4("p4_kt1", [128, 1024], BF16)
        t.p4_khg = sb4("p4_khg", [128, 1024], BF16)
        t.p4_skb = sb4("p4_skb", [128, 1024], BF16)
        t.p4_skt = [sb4(f"p4_skt{i}", [128, 1024], BF16) for i in range(2)]
        t.p4_ktt = sb4("p4_ktt", [128, 1024], BF16)
        t.p4_vpb = sb4("p4_vpb", [128, 1024], BF16)
        t.p4_vb = [sb4(f"p4_vb{i}", [128, 1024], BF16) for i in range(2)]
        t.p4_s1 = sb4("p4_s1", [128, 4096], F32)
        t.p4_s2 = sb4("p4_s2", [128, 4096], F32)
        t.p4_qtt = sb4("p4_qtt", [128, 4096], BF16)
        t.p4_sqt = sb4("p4_sqt", [128, 4096], BF16)
        t.p4_zk = sb4("p4_zk", [128, 8], F32)
        t.p4_rv = sb4("p4_rv", [128, 8], F32)
        t.p4_zq = sb4("p4_zq", [128, 32], F32)
        t.p4_rq = sb4("p4_rq", [128, 32], F32)
        t.p4_zs = sb4("p4_zs", [128, 2], F32)
        t.p4_rs = sb4("p4_rs", [128, 2], F32)
        t.p4_pb = sb4("p4_pb", [128, 384], BF16)
        t.p4_ptb = sb4("p4_ptb", [128, 512], BF16)
        t.p4_atm = sb4("p4_atm", [128, 1024], BF16)
        for i in range(2):
            B[f"p4_atm{i}"] = Buf(f"p4_atm{i}")
        t.p4_sopb = sb4("p4_sopb", [128, 1024], BF16)
        t.p4_ots = sb4("p4_ots", [128, 4096], BF16)
        esp4 = ExitStack()
        t.pS = ps("pS", [128, 1024], F32, esp4)
        if group == "B":
            t.pA = ps("pA", [128, 512], F32, esp4)
            t.pB = ps("pB", [128, 512], F32, esp4)
            t.pC = ps("pC", [128, 512], F32, esp4)
        t.pD = ps("pD", [128, 512], F32, esp4)
        t.pE = ps("pE", [128, 512], F32, esp4)
        t.pF = ps("pF", [128, 512], F32, esp4)
        if group == "B":
            p_setup(em, h, t, B, tables=True)
            em.barrier("setup_end")
        p4_attn(em, h, t, B)
        em.barrier("p4_end")
        es4.close()
        esp4.close()

    if group in ("C", "F"):
        es5 = ExitStack()
        t.otr = sb("otr", [128, HQ * N], BF16, es5)
        t.xs = sb("xs5_", [128, 8, 512], F32, es5)
        for i in range(8):
            B[f"xs{i}"] = Buf(f"xs5{i}")
        esg5 = ExitStack()
        if group == "C":
            pgz = [ps(f"pGz{i}", [128, 512], F32, esg5) for i in range(8)]
            t.pG = pgz
            for i in range(8):
                B[f"pG{i}"] = B[f"pGz{i}"]
        else:
            pgz = [ps(f"pGz{i}", [128, 512], F32, esg5) for i in range(5)]
            t.pG = [t.pA, t.pB, t.pC] + pgz
            for i in range(8):
                B[f"pG{i}"] = B[["pA", "pB", "pC", "pGz0", "pGz1", "pGz2",
                                 "pGz3", "pGz4"][i]]
        p5_out(em, h, t, B)
        es5.close()
        esg5.close()

    em.barrier("final")
    em.emit()
    es.close()
    in_names = set(ins) | set(tmp_r)
    out_names = set(outs) | (set(tmp_w) if group != "F" else set())
    if group in ("C", "F"):
        out_names = out_names | {"out"}
    return nc, in_names, out_names


_PROGS = {}


def get_programs():
    groups = ["A", "B", "C"] if SPLIT else ["F"]
    out = []
    for g in groups:
        if g not in _PROGS:
            _PROGS[g] = build_group(g)
        out.append((g,) + _PROGS[g])
    return out


def kernel(L0_attn_hidden_states, L0_attn_Wq, L0_attn_Wk, L0_attn_Wv, L0_attn_Wo,
           L0_ffn_W_base, L0_ffn_b_base, L0_ffn_W_router, L0_ffn_A_stack,
           L0_ffn_B_stack):
    x_all = np.ascontiguousarray(np.asarray(L0_attn_hidden_states, dtype=np.float32))
    shared = dict(host_constants())
    shared.update({
        "qkvW": np.ascontiguousarray(np.concatenate(
            [np.asarray(L0_attn_Wq, np.float32), np.asarray(L0_attn_Wk, np.float32),
             np.asarray(L0_attn_Wv, np.float32)], axis=1)),
        "WbT": np.ascontiguousarray(np.asarray(L0_ffn_W_base, np.float32).T),
        "Wo": np.ascontiguousarray(np.asarray(L0_attn_Wo, np.float32)),
        "Arr": np.ascontiguousarray(
            np.asarray(L0_ffn_A_stack, np.float32).transpose(2, 0, 1).reshape(D, 128)),
        "Bblk": np.ascontiguousarray(
            np.asarray(L0_ffn_B_stack, np.float32).transpose(0, 2, 1).reshape(128, D)),
        "rT": np.ascontiguousarray(np.asarray(L0_ffn_W_router, np.float32).T),
        "bb": np.ascontiguousarray(np.asarray(L0_ffn_b_base, np.float32).reshape(1, D)),
    })
    progs = get_programs()
    core_ids = list(range(NCORES))
    carry = [dict() for _ in range(NCORES)]
    for gname, nc, in_names, out_names in progs:
        in_maps = []
        for ci in range(NCORES):
            m = {}
            for k in in_names:
                if k == "x":
                    m[k] = x_all[ci]
                elif k in shared:
                    m[k] = shared[k]
                else:
                    m[k] = carry[ci][k]
            in_maps.append(m)
        res = run_bass_kernel_spmd(nc, in_maps, core_ids=core_ids)
        for ci in range(NCORES):
            for k, v in res.results[ci].items():
                carry[ci][k] = np.asarray(v)
    outs = [np.asarray(carry[ci]["out"], dtype=np.float32) for ci in range(NCORES)]
    return np.stack(outs, axis=0)


if __name__ == "__main__":
    import sys
    if "--build" in sys.argv:
        for g, nc, i_, o_ in get_programs():
            n = sum(1 for _ in nc.all_instructions())
            print(f"group {g}: {n} instructions")
